# version 03
# baseline (speedup 1.0000x reference)
import numpy as np
import contextlib
import concourse.bass as bass
import concourse.mybir as mybir
from concourse.bass_utils import run_bass_kernel_spmd
from concourse.alu_op_type import AluOpType as OP

F32 = mybir.dt.float32
BF16 = mybir.dt.bfloat16
I32 = mybir.dt.int32
AF = mybir.ActivationFunctionType
AX = mybir.AxisListType.X

D = 2048
T = 1024
HALO = 256
NH = 16
MLP = 16384
ISQ = float(1.0 / np.sqrt(128.0))
NEG = -1.0e9

ENG = ("sync", "gp", "v", "s", "t")
UNIT = {"sync": 16, "gp": 16, "v": 1, "s": 1, "t": 1}


class Prog:
    def __init__(self):
        self.steps = {e: [] for e in ENG}
        self.count = {e: 0 for e in ENG}
        self.last_w = {}
        self.readers = {}
        self.waited = {e: {} for e in ENG}

    def _wait(self, eng, src, val):
        if val <= 0 or src == eng and val <= self.count[eng]:
            pass
        if val <= 0:
            return
        if self.waited[eng].get(src, 0) >= val:
            return
        self.waited[eng][src] = val
        self.steps[eng].append(("w", src, val))

    def emit(self, eng, fn, r=(), w=(), war=()):
        for k in r:
            lw = self.last_w.get(k)
            if lw is not None:
                self._wait(eng, lw[0], lw[1])
        for k in tuple(w) + tuple(war):
            lw = self.last_w.get(k)
            if lw is not None:
                self._wait(eng, lw[0], lw[1])
            for re_, rv in self.readers.get(k, {}).items():
                self._wait(eng, re_, rv)
        self.count[eng] += UNIT[eng]
        val = self.count[eng]
        self.steps[eng].append(("i", fn))
        for k in r:
            self.readers.setdefault(k, {})[eng] = val
        for k in w:
            self.last_w[k] = (eng, val)
            self.readers[k] = {}

    def barrier(self):
        for e in ENG:
            for src in ENG:
                if src != e:
                    self._wait(e, src, self.count[src])

    def final_wait(self, eng):
        self._wait(eng, eng, self.count[eng])


def build_nc():
    nc = bass.Bass("TRN2", target_bir_lowering=False)
    dx = nc.dram_tensor("x", [T, D], F32, kind="ExternalInput").ap()
    dxh = nc.dram_tensor("xh", [HALO, D], F32, kind="ExternalInput").ap()
    dwa = nc.dram_tensor("wa", [3, NH, 128, 16, 128], F32, kind="ExternalInput").ap()
    dwoa = nc.dram_tensor("woa", [4, 128, 16, 512], F32, kind="ExternalInput").ap()
    dwb = nc.dram_tensor("wb", [4, 4, 128, 16, 512], F32, kind="ExternalInput").ap()
    dgpk = nc.dram_tensor("gpk", [128, 128, 8, 128], I32, kind="ExternalInput").ap()
    dupk = nc.dram_tensor("upk", [128, 128, 8, 128], I32, kind="ExternalInput").ap()
    dgsc = nc.dram_tensor("gsc", [128, 128, 8, 128], F32, kind="ExternalInput").ap()
    dusc = nc.dram_tensor("usc", [128, 128, 8, 128], F32, kind="ExternalInput").ap()
    ddpk = nc.dram_tensor("dpk", [64, 128, 2048], I32, kind="ExternalInput").ap()
    ddsc = nc.dram_tensor("dsc", [64, 128, 2048], F32, kind="ExternalInput").ap()
    dmask = nc.dram_tensor("mask", [4, 128, 512], F32, kind="ExternalInput").ap()
    dcosA = nc.dram_tensor("cosA", [64, 1280], F32, kind="ExternalInput").ap()
    dsinA = nc.dram_tensor("sinA", [64, 1280], F32, kind="ExternalInput").ap()
    dcosB = nc.dram_tensor("cosB", [128, 8, 64], F32, kind="ExternalInput").ap()
    dsinB = nc.dram_tensor("sinB", [128, 8, 64], F32, kind="ExternalInput").ap()
    dident = nc.dram_tensor("ident", [128, 128], F32, kind="ExternalInput").ap()
    dconst = nc.dram_tensor("cst", [128, 8], F32, kind="ExternalInput").ap()
    dout = nc.dram_tensor("o", [T, D], F32, kind="ExternalOutput").ap()
    dgpl = nc.dram_tensor("gpl", [128, 128, 2048], BF16, kind="Internal").ap()
    dupl = nc.dram_tensor("upl", [128, 128, 2048], BF16, kind="Internal").ap()
    ddpl = nc.dram_tensor("dpl", [64, 128, 4096], BF16, kind="Internal").ap()

    p = Prog()

    _nmc = [0]

    def named_sbuf(shape, dt):
        _nmc[0] += 1
        return nc.sbuf_tensor("sb%d" % _nmc[0], shape, dt)

    def named_psum(shape, dt):
        _nmc[0] += 1
        return nc.psum_tensor("ps%d" % _nmc[0], shape, dt)

    with contextlib.ExitStack() as TOP:
        E = TOP.enter_context
        xres = [E(named_sbuf([128, D], F32)) for _ in range(8)]
        xnT = E(named_sbuf([128, 16 * 1280], BF16))
        cstF = E(named_sbuf([128, 8], F32))
        cstB = E(named_sbuf([128, 8], BF16))
        idb = E(named_sbuf([128, 128], BF16))
        nmcol = E(named_sbuf([128, 4], F32))
        vcol = E(named_sbuf([128, 4], F32))
        tmpc = E(named_sbuf([128, 4], F32))
        CV = [-4.0, -6.0, 0.5, 1e-5, 1.0, 0.0, ISQ, -2.0]
        for j, v in enumerate(CV):
            nc.const_aps.register(F32, v, cstF[:, j:j + 1])
            nc.const_aps.register(BF16, v, cstB[:, j:j + 1])
        pbank = [E(named_psum([128, 512], F32)) for _ in range(6)]
        ptb = [E(named_psum([128, 1024], BF16)) for _ in range(2)]

        def XT(kt):
            return xnT[:, kt * 1280:(kt + 1) * 1280]

        p.emit("sync", lambda e: e.dma_start(out=cstF[:], in_=dconst), w=("cstF",))
        p.emit("gp", lambda e: e.dma_start(out=cstB[:], in_=dconst), w=("cstB",))
        p.emit("gp", lambda e: e.dma_start(out=idb[:], in_=dident), w=("idb",))
        for tt in range(8):
            p.emit("sync", lambda e, tt=tt: e.dma_start(out=xres[tt][:], in_=dx[tt * 128:(tt + 1) * 128, :]), w=("xr%d" % tt,))

        def ln_into_xnT(tiles, mode, sqbuf, xntok):
            for i, (sap, skey, col0) in enumerate(tiles):
                bb = i % 2
                xk = "xntok%d" % bb
                c = i % 4
                p.emit("v", lambda e, sap=sap, c=c: e.reduce_sum(out=nmcol[:, c:c + 1], in_=sap, axis=AX, negate=True), r=(skey,), w=("nm%d" % c,))
                p.emit("v", lambda e, c=c: e.tensor_single_scalar(out=nmcol[:, c:c + 1], in_=nmcol[:, c:c + 1], scalar=1.0 / D, op=OP.mult), r=("nm%d" % c,), w=("nm%d" % c,))
                p.emit("s", lambda e, sap=sap: e.activation(out=sqbuf[:], in_=sap, func=AF.Square), r=(skey, "cstF"), w=("sqb",))
                p.emit("v", lambda e, c=c, sqbuf=sqbuf: e.reduce_sum(out=vcol[:, c:c + 1], in_=sqbuf[:], axis=AX), r=("sqb",), w=("vc%d" % c,))
                p.emit("v", lambda e, c=c: e.tensor_single_scalar(out=vcol[:, c:c + 1], in_=vcol[:, c:c + 1], scalar=1.0 / D, op=OP.mult), r=("vc%d" % c,), w=("vc%d" % c,))
                p.emit("v", lambda e, c=c: e.tensor_tensor(out=tmpc[:, 0:1], in0=nmcol[:, c:c + 1], in1=nmcol[:, c:c + 1], op=OP.mult), r=("nm%d" % c,), w=("tmpcc",))
                p.emit("v", lambda e, c=c: e.tensor_tensor(out=vcol[:, c:c + 1], in0=vcol[:, c:c + 1], in1=tmpc[:, 0:1], op=OP.subtract), r=("vc%d" % c, "tmpcc"), w=("vc%d" % c,))
                p.emit("s", lambda e, c=c: e.activation(out=vcol[:, c:c + 1], in_=vcol[:, c:c + 1], func=AF.Sqrt, bias=1e-5, scale=1.0), r=("vc%d" % c, "cstF"), w=("vc%d" % c,))
                p.emit("v", lambda e, c=c: e.reciprocal(out=vcol[:, c:c + 1], in_=vcol[:, c:c + 1]), r=("vc%d" % c,), w=("vc%d" % c,))
                p.emit("v", lambda e, sap=sap, c=c, bb=bb, xntok=xntok: e.tensor_scalar(out=xntok[bb][:], in0=sap, scalar1=nmcol[:, c:c + 1], scalar2=vcol[:, c:c + 1], op0=OP.add, op1=OP.mult), r=(skey, "nm%d" % c, "vc%d" % c), w=(xk,))
                for j in range(16):
                    pt = j % 2
                    if mode == "nat":
                        src = xntok[bb][:, j * 128:(j + 1) * 128]
                    else:
                        jt, par = j % 8, j // 8
                        src = xntok[bb][:, jt * 256 + par: jt * 256 + 256: 2]
                    p.emit("t", lambda e, src=src, pt=pt: e.transpose(ptb[pt][:, 0:128], src, idb[:]), r=(xk, "idb"), w=("ptb%d" % pt,))
                    p.emit("s", lambda e, j=j, col0=col0, pt=pt: e.copy(out=XT(j)[:, col0:col0 + 128], in_=ptb[pt][:, 0:128]), r=("ptb%d" % pt,), w=("xnT_%d_%d" % (j, col0),))

        def xnT_keys(kts, c0, c1):
            ks = []
            for j in kts:
                for col0 in range(0, 1280, 128):
                    if c0 <= col0 < c1:
                        ks.append("xnT_%d_%d" % (j, col0))
            return tuple(ks)

        with contextlib.ExitStack() as S:
            sqbuf = S.enter_context(named_sbuf([128, D], F32))
            xntok = [S.enter_context(named_sbuf([128, D], BF16)) for _ in range(2)]
            xhb = [S.enter_context(named_sbuf([128, D], F32)) for _ in range(2)]
            for i in range(2):
                p.emit("sync", lambda e, i=i: e.dma_start(out=xhb[i][:], in_=dxh[i * 128:(i + 1) * 128, :]), w=("xh%d" % i,))
            tiles = [(xhb[i][:], "xh%d" % i, i * 128) for i in range(2)] + \
                    [(xres[tt][:], "xr%d" % tt, 256 + tt * 128) for tt in range(8)]
            ln_into_xnT(tiles, "nat", sqbuf, xntok)
            p.barrier()

        with contextlib.ExitStack() as S:
            A = S.enter_context
            wqb = A(named_sbuf([128, 16 * 128], BF16))
            wkb = A(named_sbuf([128, 16 * 128], BF16))
            wvb = A(named_sbuf([128, 16 * 128], BF16))
            q1buf = A(named_sbuf([64, 768], F32))
            q2buf = A(named_sbuf([64, 768], F32))
            rba = A(named_sbuf([64, 768], F32))
            rbb = A(named_sbuf([64, 768], F32))
            r1b = A(named_sbuf([64, 768], BF16))
            r2b = A(named_sbuf([64, 768], BF16))
            qT = A(named_sbuf([128, 512], BF16))
            kT = A(named_sbuf([128, 768], BF16))
            vT = A(named_sbuf([128, 768], BF16))
            vtok = A(named_sbuf([128, 6 * 128], BF16))
            outT = A(named_sbuf([128, 16 * 512], BF16))
            smb = A(named_sbuf([128, 512], F32))
            prb = A(named_sbuf([128, 512], BF16))
            pTb = A(named_sbuf([128, 512], BF16))
            stat = A(named_sbuf([128, 4], F32))
            maskb = A(named_sbuf([128, 4 * 512], F32))
            cosA = A(named_sbuf([64, 1280], F32))
            sinA = A(named_sbuf([64, 1280], F32))
            woa = A(named_sbuf([128, 16 * 512], BF16))
            for i in range(4):
                p.emit("sync", lambda e, i=i: e.dma_start(out=maskb[:, i * 512:(i + 1) * 512], in_=dmask[i]), w=("mask",))
            p.emit("sync", lambda e: e.dma_start(out=cosA[:], in_=dcosA), w=("cosA",))
            p.emit("sync", lambda e: e.dma_start(out=sinA[:], in_=dsinA), w=("sinA",))

            for half in range(2):
                kc0 = half * 512
                qc0 = 256 + half * 512
                for h in range(NH):
                    for mi, (wb_, key) in enumerate(((wqb, "wq"), (wkb, "wk"), (wvb, "wv"))):
                        p.emit("gp", lambda e, mi=mi, h=h, wb_=wb_: e.dma_start(
                            out=wb_[:].rearrange("p (kt n) -> p kt n", kt=16), in_=dwa[mi, h]), w=(key,))

                    def proj(wkey, wb_, cols, c0, pb0, outkey, bf_direct):
                        done = 0
                        bi = 0
                        while done < cols:
                            nsz = min(512, cols - done)
                            bank = pbank[pb0 + (bi % 2)]
                            bkey = "pb%d" % (pb0 + (bi % 2))
                            bi += 1
                            for kt in range(16):
                                p.emit("t", lambda e, kt=kt, wb_=wb_, c0=c0, done=done, nsz=nsz, bank=bank:
                                       e.matmul(bank[:, 0:nsz], wb_[:, kt * 128:(kt + 1) * 128], XT(kt)[:, c0 + done:c0 + done + nsz],
                                                start=(kt == 0), stop=(kt == 15)),
                                       r=(wkey,) + xnT_keys([kt], c0 + done, c0 + done + nsz),
                                       w=(bkey,) if kt == 15 else (), war=(bkey,) if kt == 0 else ())
                            if bf_direct is not None:
                                p.emit("s", lambda e, bank=bank, done=done, nsz=nsz, bf_direct=bf_direct: e.copy(out=bf_direct[:, done:done + nsz], in_=bank[:, 0:nsz]), r=(bkey,), w=(outkey,))
                            else:
                                p.emit("s", lambda e, bank=bank, done=done, nsz=nsz: e.copy(out=q1buf[:, done:done + nsz], in_=bank[0:64, 0:nsz]), r=(bkey,), w=(outkey,))
                                p.emit("s", lambda e, bank=bank, done=done, nsz=nsz: e.copy(out=q2buf[:, done:done + nsz], in_=bank[64:128, 0:nsz]), r=(bkey,), w=(outkey,))
                            done += nsz

                    def rope(key, ncols, ccol0, outTb, okey):
                        cS = cosA[:, ccol0:ccol0 + ncols]
                        sS = sinA[:, ccol0:ccol0 + ncols]
                        p1 = q1buf[:, 0:ncols]
                        p2 = q2buf[:, 0:ncols]
                        p.emit("v", lambda e: e.tensor_tensor(out=rba[:, 0:ncols], in0=p1, in1=cS, op=OP.mult), r=(key, "cosA"), w=("rba",))
                        p.emit("v", lambda e: e.tensor_tensor(out=rbb[:, 0:ncols], in0=p2, in1=sS, op=OP.mult), r=(key, "sinA"), w=("rbb",))
                        p.emit("v", lambda e: e.tensor_tensor(out=r1b[:, 0:ncols], in0=rba[:, 0:ncols], in1=rbb[:, 0:ncols], op=OP.subtract), r=("rba", "rbb"), w=("r1b",))
                        p.emit("v", lambda e: e.tensor_tensor(out=rba[:, 0:ncols], in0=p2, in1=cS, op=OP.mult), r=(key, "cosA"), w=("rba",))
                        p.emit("v", lambda e: e.tensor_tensor(out=rbb[:, 0:ncols], in0=p1, in1=sS, op=OP.mult), r=(key, "sinA"), w=("rbb",))
                        p.emit("v", lambda e: e.tensor_tensor(out=r2b[:, 0:ncols], in0=rba[:, 0:ncols], in1=rbb[:, 0:ncols], op=OP.add), r=("rba", "rbb"), w=("r2b",))
                        p.emit("sync", lambda e: e.dma_start(out=outTb[0:64, 0:ncols], in_=r1b[:, 0:ncols]), r=("r1b",), w=(okey,))
                        p.emit("sync", lambda e: e.dma_start(out=outTb[64:128, 0:ncols], in_=r2b[:, 0:ncols]), r=("r2b",), w=(okey,))

                    proj("wq", wqb, 512, qc0, 0, "qfull", None)
                    rope("qfull", 512, 256 + half * 512, qT, "qT")
                    proj("wk", wkb, 768, kc0, 0, "qfull", None)
                    rope("qfull", 768, half * 512, kT, "kT")
                    proj("wv", wvb, 768, kc0, 2, "vT", vT)
                    for i in range(6):
                        pt = i % 2
                        p.emit("t", lambda e, i=i, pt=pt: e.transpose(ptb[pt][:, 0:128], vT[:, i * 128:(i + 1) * 128], idb[:]), r=("vT", "idb"), w=("ptb%d" % pt,))
                        p.emit("s", lambda e, i=i, pt=pt: e.copy(out=vtok[:, i * 128:(i + 1) * 128], in_=ptb[pt][:, 0:128]), r=("ptb%d" % pt,), w=("vtok",))
                    for ci in range(2):
                        glob_first = (half == 0 and ci == 0)
                        for qs in range(2):
                            qi = ci * 256 + qs * 128
                            bank = pbank[4]
                            bkey = "pb4"
                            p.emit("t", lambda e, qi=qi, ci=ci, bank=bank: e.matmul(bank[:], qT[:, qi:qi + 128], kT[:, ci * 256:ci * 256 + 512], start=True, stop=True), r=("qT", "kT"), w=(bkey,))
                            mi = (0 if glob_first else 2) + qs
                            p.emit("v", lambda e, mi=mi, bank=bank: e.tensor_tensor(out=smb[:], in0=bank[:], in1=maskb[:, mi * 512:(mi + 1) * 512], op=OP.add), r=(bkey, "mask"), w=("smb",))
                            p.emit("v", lambda e: e.reduce_max(out=stat[:, 0:1], in_=smb[:], axis=AX, negate=True), r=("smb",), w=("st0",))
                            p.emit("v", lambda e: e.tensor_single_scalar(out=stat[:, 1:2], in_=stat[:, 0:1], scalar=ISQ, op=OP.mult), r=("st0",), w=("st1",))
                            p.emit("s", lambda e: e.activation(out=prb[:], in_=smb[:], func=AF.Exp, bias=stat[:, 1:2], scale=ISQ), r=("smb", "st1", "cstF"), w=("prb",))
                            p.emit("v", lambda e: e.reduce_sum(out=stat[:, 2:3], in_=prb[:], axis=AX), r=("prb",), w=("st2",))
                            p.emit("v", lambda e: e.reciprocal(out=stat[:, 2:3], in_=stat[:, 2:3]), r=("st2",), w=("st2",))
                            p.emit("v", lambda e: e.tensor_single_scalar(out=prb[:], in_=prb[:], scalar=stat[:, 2:3], op=OP.mult), r=("prb", "st2"), w=("prb",))
                            for ks in range(4):
                                pt = ks % 2
                                p.emit("t", lambda e, ks=ks, pt=pt: e.transpose(ptb[pt][:, 0:128], prb[:, ks * 128:(ks + 1) * 128], idb[:]), r=("prb", "idb"), w=("ptb%d" % pt,))
                                p.emit("s", lambda e, ks=ks, pt=pt: e.copy(out=pTb[:, ks * 128:(ks + 1) * 128], in_=ptb[pt][:, 0:128]), r=("ptb%d" % pt,), w=("pTb",))
                            bank2 = pbank[5]
                            bkey2 = "pb5"
                            for ks in range(4):
                                p.emit("t", lambda e, ks=ks, ci=ci, bank2=bank2: e.matmul(bank2[:, 0:128], vtok[:, (ci * 2 + ks) * 128:(ci * 2 + ks + 1) * 128], pTb[:, ks * 128:(ks + 1) * 128], start=(ks == 0), stop=(ks == 3)), r=("vtok", "pTb"), w=(bkey2,) if ks == 3 else (), war=(bkey2,) if ks == 0 else ())
                            p.emit("s", lambda e, h=h, qi=qi, bank2=bank2: e.copy(out=outT[:, h * 512 + qi:h * 512 + qi + 128], in_=bank2[:, 0:128]), r=(bkey2,), w=("outT",))
                for oc in range(4):
                    p.emit("gp", lambda e, oc=oc: e.dma_start(out=woa[:].rearrange("p (kt n) -> p kt n", kt=16), in_=dwoa[oc]), w=("woa",))
                    for tt4 in range(4):
                        tt = half * 4 + tt4
                        bank = pbank[tt4 % 2]
                        bkey = "pb%d" % (tt4 % 2)
                        for h in range(NH):
                            p.emit("t", lambda e, h=h, tt4=tt4, bank=bank: e.matmul(bank[:], outT[:, h * 512 + tt4 * 128:h * 512 + tt4 * 128 + 128], woa[:, h * 512:(h + 1) * 512], start=(h == 0), stop=(h == 15)), r=("outT", "woa"), w=(bkey,) if h == 15 else (), war=(bkey,) if h == 0 else ())
                        p.emit("v", lambda e, tt=tt, oc=oc, bank=bank: e.tensor_tensor(out=xres[tt][:, oc * 512:(oc + 1) * 512], in0=xres[tt][:, oc * 512:(oc + 1) * 512], in1=bank[:], op=OP.add), r=(bkey, "xr%d" % tt), w=("xr%d" % tt,))
            p.barrier()

        def ffn(passno):
            with contextlib.ExitStack() as S:
                with contextlib.ExitStack() as SL:
                    sqbuf = SL.enter_context(named_sbuf([128, D], F32))
                    xntok = [SL.enter_context(named_sbuf([128, D], BF16)) for _ in range(2)]
                    tiles = [(xres[tt][:], "xr%d" % tt, 256 + tt * 128) for tt in range(8)]
                    ln_into_xnT(tiles, "eo", sqbuf, xntok)
                    p.barrier()
                A = S.enter_context
                pkg = A(named_sbuf([128, 1024], I32))
                pku = A(named_sbuf([128, 1024], I32))
                pkd = A(named_sbuf([128, 2048], I32))
                scbuf = A(named_sbuf([128, 2048], F32))
                tmi = A(named_sbuf([128, 1024], I32))
                tgi = A(named_sbuf([128, 1024], I32))
                mb = A(named_sbuf([128, 1024], BF16))
                t1b = A(named_sbuf([128, 1024], BF16))
                t2b = A(named_sbuf([128, 1024], BF16))
                sgb = A(named_sbuf([128, 1024], BF16))
                glo = A(named_sbuf([128, 1024], BF16))
                ghi = A(named_sbuf([128, 1024], BF16))
                ulo = A(named_sbuf([128, 1024], BF16))
                uhi = A(named_sbuf([128, 1024], BF16))
                dql = [A(named_sbuf([128, 2048], BF16)) for _ in range(2)]
                dqh = [A(named_sbuf([128, 2048], BF16)) for _ in range(2)]
                gsb = A(named_sbuf([128, 1024], BF16))
                usb = A(named_sbuf([128, 1024], BF16))
                inter = [A(named_sbuf([128, 1024], BF16)) for _ in range(4)]

                def decode(pk_ap, pk_key, sc_ap, w_, dst, dkey, hiside, sckey="scbuf"):
                    if hiside == 0:
                        p.emit("v", lambda e: e.tensor_single_scalar(out=tmi[:, 0:w_], in_=pk_ap, scalar=7, op=OP.bitwise_and), r=(pk_key,), w=("tmi",))
                        p.emit("v", lambda e: e.tensor_scalar(out=tgi[:, 0:w_], in0=pk_ap, scalar1=3, scalar2=1, op0=OP.logical_shift_right, op1=OP.bitwise_and), r=(pk_key,), w=("tgi",))
                    else:
                        p.emit("v", lambda e: e.tensor_scalar(out=tmi[:, 0:w_], in0=pk_ap, scalar1=4, scalar2=7, op0=OP.logical_shift_right, op1=OP.bitwise_and), r=(pk_key,), w=("tmi",))
                        p.emit("v", lambda e: e.tensor_single_scalar(out=tgi[:, 0:w_], in_=pk_ap, scalar=7, op=OP.logical_shift_right), r=(pk_key,), w=("tgi",))
                    p.emit("s", lambda e: e.activation(out=mb[:, 0:w_], in_=tmi[:, 0:w_], func=AF.Copy), r=("tmi", "cstF"), w=("mbb",))
                    p.emit("s", lambda e: e.activation(out=t1b[:, 0:w_], in_=tmi[:, 0:w_], func=AF.Relu, bias=-4.0, scale=1.0), r=("tmi", "cstF"), w=("t1b",))
                    p.emit("s", lambda e: e.activation(out=t2b[:, 0:w_], in_=tmi[:, 0:w_], func=AF.Relu, bias=-6.0, scale=1.0), r=("tmi", "cstF"), w=("t2b",))
                    p.emit("v", lambda e: e.tensor_scalar(out=sgb[:, 0:w_], in0=tgi[:, 0:w_], scalar1=-1, scalar2=0.5, op0=OP.mult, op1=OP.add), r=("tgi",), w=("sgb",))
                    p.emit("v", lambda e: e.tensor_tensor(out=t1b[:, 0:w_], in0=mb[:, 0:w_], in1=t1b[:, 0:w_], op=OP.add), r=("mbb", "t1b"), w=("t1b",))
                    p.emit("v", lambda e: e.scalar_tensor_tensor(out=t2b[:, 0:w_], in0=t2b[:, 0:w_], scalar=2.0, in1=t1b[:, 0:w_], op0=OP.mult, op1=OP.add), r=("t2b", "t1b"), w=("t2b",))
                    p.emit("v", lambda e: e.tensor_tensor(out=sgb[:, 0:w_], in0=t2b[:, 0:w_], in1=sgb[:, 0:w_], op=OP.mult), r=("t2b", "sgb"), w=("sgb",))
                    p.emit("v", lambda e: e.tensor_tensor(out=dst, in0=sgb[:, 0:w_], in1=sc_ap, op=OP.mult), r=("sgb", sckey), w=(dkey,))

                def load_or_dequant_gu(m):
                    if passno == 0:
                        p.emit("sync", lambda e, m=m: e.dma_start(out=pkg[:].rearrange("p (jt n) -> p jt n", jt=8), in_=dgpk[m]), w=("pkg",))
                        p.emit("sync", lambda e, m=m: e.dma_start(out=scbuf[:, 0:1024].rearrange("p (jt n) -> p jt n", jt=8), in_=dgsc[m]), w=("scbuf",))
                        p.emit("sync", lambda e, m=m: e.dma_start(out=pku[:].rearrange("p (jt n) -> p jt n", jt=8), in_=dupk[m]), w=("pku",))
                        p.emit("sync", lambda e, m=m: e.dma_start(out=scbuf[:, 1024:2048].rearrange("p (jt n) -> p jt n", jt=8), in_=dusc[m]), w=("scbuf",))
                        decode(pkg[:], "pkg", scbuf[:, 0:1024], 1024, glo[:], "glo", 0)
                        decode(pkg[:], "pkg", scbuf[:, 0:1024], 1024, ghi[:], "ghi", 1)
                        decode(pku[:], "pku", scbuf[:, 1024:2048], 1024, ulo[:], "ulo", 0)
                        decode(pku[:], "pku", scbuf[:, 1024:2048], 1024, uhi[:], "uhi", 1)
                        p.emit("gp", lambda e, m=m: e.dma_start(out=dgpl[m, :, 0:1024], in_=glo[:]), r=("glo",), w=("dgpl%d" % m,))
                        p.emit("gp", lambda e, m=m: e.dma_start(out=dgpl[m, :, 1024:2048], in_=ghi[:]), r=("ghi",), w=("dgpl%d" % m,))
                        p.emit("gp", lambda e, m=m: e.dma_start(out=dupl[m, :, 0:1024], in_=ulo[:]), r=("ulo",), w=("dupl%d" % m,))
                        p.emit("gp", lambda e, m=m: e.dma_start(out=dupl[m, :, 1024:2048], in_=uhi[:]), r=("uhi",), w=("dupl%d" % m,))
                    else:
                        p.emit("sync", lambda e, m=m: e.dma_start(out=glo[:], in_=dgpl[m, :, 0:1024]), r=("dgpl%d" % m,), w=("glo",))
                        p.emit("sync", lambda e, m=m: e.dma_start(out=ghi[:], in_=dgpl[m, :, 1024:2048]), r=("dgpl%d" % m,), w=("ghi",))
                        p.emit("sync", lambda e, m=m: e.dma_start(out=ulo[:], in_=dupl[m, :, 0:1024]), r=("dupl%d" % m,), w=("ulo",))
                        p.emit("sync", lambda e, m=m: e.dma_start(out=uhi[:], in_=dupl[m, :, 1024:2048]), r=("dupl%d" % m,), w=("uhi",))

                def load_or_dequant_d(jt, slot):
                    if passno == 0:
                        p.emit("sync", lambda e, jt=jt: e.dma_start(out=pkd[:], in_=ddpk[jt]), w=("pkd",))
                        p.emit("sync", lambda e, jt=jt: e.dma_start(out=scbuf[:], in_=ddsc[jt]), w=("scbuf",))
                        for half in range(2):
                            sl = slice(half * 1024, (half + 1) * 1024)
                            decode(pkd[:, sl], "pkd", scbuf[:, sl], 1024, dql[slot][:, sl], "dql%d" % slot, 0)
                            decode(pkd[:, sl], "pkd", scbuf[:, sl], 1024, dqh[slot][:, sl], "dqh%d" % slot, 1)
                        p.emit("gp", lambda e, jt=jt, slot=slot: e.dma_start(out=ddpl[jt, :, 0:2048], in_=dql[slot][:]), r=("dql%d" % slot,), w=("ddpl%d" % jt,))
                        p.emit("gp", lambda e, jt=jt, slot=slot: e.dma_start(out=ddpl[jt, :, 2048:4096], in_=dqh[slot][:]), r=("dqh%d" % slot,), w=("ddpl%d" % jt,))
                    else:
                        p.emit("sync", lambda e, jt=jt, slot=slot: e.dma_start(out=dql[slot][:], in_=ddpl[jt, :, 0:2048]), r=("ddpl%d" % jt,), w=("dql%d" % slot,))
                        p.emit("sync", lambda e, jt=jt, slot=slot: e.dma_start(out=dqh[slot][:], in_=ddpl[jt, :, 2048:4096]), r=("ddpl%d" % jt,), w=("dqh%d" % slot,))

                for g in range(32):
                    ms = [g * 2, g * 2 + 1, 64 + g * 2, 64 + g * 2 + 1]
                    for ii, m in enumerate(ms):
                        load_or_dequant_gu(m)
                        for th in range(2):
                            bg = pbank[th]
                            bgk = "pb%d" % th
                            bu = pbank[2 + th]
                            buk = "pb%d" % (2 + th)
                            for wpl, wkeys, bank, bkey in ((0, ("glo", "ghi"), bg, bgk), (1, ("ulo", "uhi"), bu, buk)):
                                planes = (glo, ghi) if wpl == 0 else (ulo, uhi)
                                for j in range(16):
                                    jt, par = j % 8, j // 8
                                    wsrc = planes[par][:, jt * 128:(jt + 1) * 128]
                                    p.emit("t", lambda e, wsrc=wsrc, jt=jt, par=par, th=th, bank=bank, j=j:
                                           e.matmul(bank[:], wsrc, XT(par * 8 + jt)[:, 256 + th * 512:256 + th * 512 + 512], start=(j == 0), stop=(j == 15)),
                                           r=(wkeys[par],) + xnT_keys([par * 8 + jt], 256, 1280), w=(bkey,) if j == 15 else (), war=(bkey,) if j == 0 else ())
                            p.emit("s", lambda e, th=th, bg=bg: e.activation(out=gsb[:, th * 512:(th + 1) * 512], in_=bg[:], func=AF.Gelu_apprx_tanh), r=(bgk, "cstF"), w=("gsb",))
                            p.emit("s", lambda e, th=th, bu=bu: e.copy(out=usb[:, th * 512:(th + 1) * 512], in_=bu[:]), r=(buk,), w=("usb",))
                        p.emit("v", lambda e, ii=ii: e.tensor_tensor(out=inter[ii][:], in0=gsb[:], in1=usb[:], op=OP.mult), r=("gsb", "usb"), w=("int%d" % ii,))
                    load_or_dequant_d(g * 2, 0)
                    load_or_dequant_d(g * 2 + 1, 1)
                    for tt in range(8):
                        for hc in range(4):
                            bank = pbank[4 + (hc % 2)]
                            bkey = "pb%d" % (4 + (hc % 2))
                            mmlist = [(0, dql[0], "dql0"), (2, dqh[0], "dqh0"), (1, dql[1], "dql1"), (3, dqh[1], "dqh1")]
                            for zi, (ii, wpl, wplk) in enumerate(mmlist):
                                p.emit("t", lambda e, ii=ii, wpl=wpl, tt=tt, hc=hc, bank=bank, zi=zi:
                                       e.matmul(bank[:], inter[ii][:, tt * 128:(tt + 1) * 128], wpl[:, hc * 512:(hc + 1) * 512], start=(zi == 0), stop=(zi == 3)),
                                       r=("int%d" % ii, wplk), w=(bkey,) if zi == 3 else (), war=(bkey,) if zi == 0 else ())
                            p.emit("v", lambda e, tt=tt, hc=hc, bank=bank: e.tensor_tensor(out=xres[tt][:, hc * 512:(hc + 1) * 512], in0=xres[tt][:, hc * 512:(hc + 1) * 512], in1=bank[:], op=OP.add), r=(bkey, "xr%d" % tt), w=("xr%d" % tt,))
                p.barrier()

        ffn(0)

        with contextlib.ExitStack() as S:
            with contextlib.ExitStack() as SL:
                sqbuf = SL.enter_context(named_sbuf([128, D], F32))
                xntok = [SL.enter_context(named_sbuf([128, D], BF16)) for _ in range(2)]
                tiles = [(xres[tt][:], "xr%d" % tt, 256 + tt * 128) for tt in range(8)]
                ln_into_xnT(tiles, "nat", sqbuf, xntok)
                p.barrier()
            A = S.enter_context
            qtok = [A(named_sbuf([128, D], BF16)) for _ in range(2)]
            ktok = [A(named_sbuf([128, D], BF16)) for _ in range(2)]
            vtokb = [A(named_sbuf([128, D], BF16)) for _ in range(2)]
            aTb = A(named_sbuf([128, 16 * 1024], BF16))
            wbb = A(named_sbuf([128, 8 * 512], BF16))
            wbb2 = A(named_sbuf([128, 8 * 512], BF16))
            qs32 = A(named_sbuf([128, 512], F32))
            ra = A(named_sbuf([128, 256], F32))
            rb2 = A(named_sbuf([128, 256], F32))
            cosB = A(named_sbuf([128, 8 * 64], F32))
            sinB = A(named_sbuf([128, 8 * 64], F32))
            sc_f = A(named_sbuf([128, 256], F32))
            pr_b = A(named_sbuf([128, 256], BF16))
            tmpb = A(named_sbuf([128, D], F32))
            stat2 = A(named_sbuf([128, 4], F32))
            p.emit("sync", lambda e: e.dma_start(out=cosB[:].rearrange("p (tt f) -> p tt f", tt=8), in_=dcosB), w=("cosB",))
            p.emit("sync", lambda e: e.dma_start(out=sinB[:].rearrange("p (tt f) -> p tt f", tt=8), in_=dsinB), w=("sinB",))
            for half in range(4):
                for mi, dest, dkeyp, dorope in ((0, qtok, "qt", True), (1, ktok, "kt_", True), (2, vtokb, "vt", False)):
                    for oc in range(4):
                        p.emit("gp", lambda e, mi=mi, oc=oc: e.dma_start(out=wbb[:].rearrange("p (kt n) -> p kt n", kt=8), in_=dwb[mi, oc, :, 0:8, :]), w=("wbb",))
                        p.emit("gp", lambda e, mi=mi, oc=oc: e.dma_start(out=wbb2[:].rearrange("p (kt n) -> p kt n", kt=8), in_=dwb[mi, oc, :, 8:16, :]), w=("wbb2",))
                        for lt in range(2):
                            tt = half * 2 + lt
                            bank = pbank[lt % 2]
                            bkey = "pb%d" % (lt % 2)
                            for kt in range(16):
                                wsel = wbb if kt < 8 else wbb2
                                wselk = "wbb" if kt < 8 else "wbb2"
                                p.emit("t", lambda e, kt=kt, tt=tt, bank=bank, wsel=wsel: e.matmul(bank[:], XT(kt)[:, 256 + tt * 128: 256 + tt * 128 + 128], wsel[:, (kt % 8) * 512:(kt % 8 + 1) * 512], start=(kt == 0), stop=(kt == 15)),
                                       r=(wselk,) + xnT_keys([kt], 256 + tt * 128, 256 + tt * 128 + 128), w=(bkey,) if kt == 15 else (), war=(bkey,) if kt == 0 else ())
                            if not dorope:
                                p.emit("s", lambda e, lt=lt, oc=oc, dest=dest, bank=bank: e.copy(out=dest[lt][:, oc * 512:(oc + 1) * 512], in_=bank[:]), r=(bkey,), w=(dkeyp + str(lt),))
                            else:
                                p.emit("s", lambda e, bank=bank: e.copy(out=qs32[:], in_=bank[:]), r=(bkey,), w=("qs32",))
                                cS = cosB[:, tt * 64:(tt + 1) * 64][:, None, :].broadcast_to((128, 4, 64))
                                sS = sinB[:, tt * 64:(tt + 1) * 64][:, None, :].broadcast_to((128, 4, 64))
                                q3 = qs32[:].rearrange("p (h d) -> p h d", h=4)
                                p.emit("v", lambda e, cS=cS, q3=q3: e.tensor_tensor(out=ra[:].rearrange("p (h d) -> p h d", h=4), in0=q3[:, :, 0:64], in1=cS, op=OP.mult), r=("qs32", "cosB"), w=("ra",))
                                p.emit("v", lambda e, sS=sS, q3=q3: e.tensor_tensor(out=rb2[:].rearrange("p (h d) -> p h d", h=4), in0=q3[:, :, 64:128], in1=sS, op=OP.mult), r=("qs32", "sinB"), w=("rb2",))
                                d3 = dest[lt][:, oc * 512:(oc + 1) * 512].rearrange("p (h d) -> p h d", h=4)
                                p.emit("v", lambda e, d3=d3: e.tensor_tensor(out=d3[:, :, 0:64], in0=ra[:].rearrange("p (h d) -> p h d", h=4), in1=rb2[:].rearrange("p (h d) -> p h d", h=4), op=OP.subtract), r=("ra", "rb2"), w=(dkeyp + str(lt),))
                                p.emit("v", lambda e, cS=cS, q3=q3: e.tensor_tensor(out=ra[:].rearrange("p (h d) -> p h d", h=4), in0=q3[:, :, 64:128], in1=cS, op=OP.mult), r=("qs32", "cosB"), w=("ra",))
                                p.emit("v", lambda e, sS=sS, q3=q3: e.tensor_tensor(out=rb2[:].rearrange("p (h d) -> p h d", h=4), in0=q3[:, :, 0:64], in1=sS, op=OP.mult), r=("qs32", "sinB"), w=("rb2",))
                                p.emit("v", lambda e, d3=d3: e.tensor_tensor(out=d3[:, :, 64:128], in0=ra[:].rearrange("p (h d) -> p h d", h=4), in1=rb2[:].rearrange("p (h d) -> p h d", h=4), op=OP.add), r=("ra", "rb2"), w=(dkeyp + str(lt),))
                for lt in range(2):
                    tt = half * 2 + lt
                    for h in range(NH):
                        qb = qtok[lt][:, h * 128:(h + 1) * 128][:, None, :].broadcast_to((128, 16, 128))
                        p.emit("v", lambda e, lt=lt, qb=qb: e.tensor_tensor(out=tmpb[:].rearrange("p (t d) -> p t d", t=16), in0=ktok[lt][:].rearrange("p (t d) -> p t d", t=16), in1=qb, op=OP.mult), r=("qt%d" % lt, "kt_%d" % lt), w=("tmpb",))
                        p.emit("v", lambda e, h=h: e.reduce_sum(out=sc_f[:, h * 16:(h + 1) * 16], in_=tmpb[:].rearrange("p (t d) -> p t d", t=16), axis=AX), r=("tmpb",), w=("scf",))
                    for h in range(NH):
                        hs = slice(h * 16, (h + 1) * 16)
                        p.emit("v", lambda e, hs=hs: e.reduce_max(out=stat2[:, 0:1], in_=sc_f[:, hs], axis=AX, negate=True), r=("scf",), w=("s20",))
                        p.emit("v", lambda e: e.tensor_single_scalar(out=stat2[:, 1:2], in_=stat2[:, 0:1], scalar=ISQ, op=OP.mult), r=("s20",), w=("s21",))
                        p.emit("s", lambda e, hs=hs: e.activation(out=pr_b[:, hs], in_=sc_f[:, hs], func=AF.Exp, bias=stat2[:, 1:2], scale=ISQ), r=("scf", "s21", "cstF"), w=("prb2",))
                        p.emit("v", lambda e, hs=hs: e.reduce_sum(out=stat2[:, 2:3], in_=pr_b[:, hs], axis=AX), r=("prb2",), w=("s22",))
                        p.emit("v", lambda e: e.reciprocal(out=stat2[:, 2:3], in_=stat2[:, 2:3]), r=("s22",), w=("s22",))
                        p.emit("v", lambda e, hs=hs: e.tensor_single_scalar(out=pr_b[:, hs], in_=pr_b[:, hs], scalar=stat2[:, 2:3], op=OP.mult), r=("prb2", "s22"), w=("prb2",))
                        pb3 = pr_b[:, hs][:, None, :].broadcast_to((128, 128, 16))
                        v3 = vtokb[lt][:].rearrange("p (t d) -> p d t", t=16)
                        p.emit("v", lambda e, pb3=pb3, v3=v3: e.tensor_tensor(out=tmpb[:].rearrange("p (d t) -> p d t", d=128), in0=v3, in1=pb3, op=OP.mult), r=("prb2", "vt%d" % lt), w=("tmpb",))
                        p.emit("v", lambda e: e.reduce_sum(out=qs32[:, 0:128], in_=tmpb[:].rearrange("p (d t) -> p d t", d=128), axis=AX), r=("tmpb",), w=("qs32",))
                        p.emit("s", lambda e, lt=lt, h=h: e.copy(out=qtok[lt][:, h * 128:(h + 1) * 128], in_=qs32[:, 0:128]), r=("qs32",), w=("at%d" % lt,), war=("qt%d" % lt,))
                for lt in range(2):
                    tt = half * 2 + lt
                    for kt in range(16):
                        pt = kt % 2
                        p.emit("t", lambda e, lt=lt, kt=kt, pt=pt: e.transpose(ptb[pt][:, 0:128], qtok[lt][:, kt * 128:(kt + 1) * 128], idb[:]), r=("at%d" % lt, "idb"), w=("ptb%d" % pt,))
                        p.emit("s", lambda e, tt=tt, kt=kt, pt=pt: e.copy(out=aTb[:, kt * 1024 + tt * 128: kt * 1024 + tt * 128 + 128], in_=ptb[pt][:, 0:128]), r=("ptb%d" % pt,), w=("aT%d" % kt,))
            for oc in range(4):
                p.emit("gp", lambda e, oc=oc: e.dma_start(out=wbb[:].rearrange("p (kt n) -> p kt n", kt=8), in_=dwb[3, oc, :, 0:8, :]), w=("wbb",))
                p.emit("gp", lambda e, oc=oc: e.dma_start(out=wbb2[:].rearrange("p (kt n) -> p kt n", kt=8), in_=dwb[3, oc, :, 8:16, :]), w=("wbb2",))
                for tt in range(8):
                    bank = pbank[tt % 2]
                    bkey = "pb%d" % (tt % 2)
                    for kt in range(16):
                        wsel = wbb if kt < 8 else wbb2
                        wselk = "wbb" if kt < 8 else "wbb2"
                        p.emit("t", lambda e, kt=kt, tt=tt, bank=bank, wsel=wsel: e.matmul(bank[:], aTb[:, kt * 1024 + tt * 128: kt * 1024 + tt * 128 + 128], wsel[:, (kt % 8) * 512:(kt % 8 + 1) * 512], start=(kt == 0), stop=(kt == 15)), r=(wselk, "aT%d" % kt), w=(bkey,) if kt == 15 else (), war=(bkey,) if kt == 0 else ())
                    p.emit("v", lambda e, tt=tt, oc=oc, bank=bank: e.tensor_tensor(out=xres[tt][:, oc * 512:(oc + 1) * 512], in0=xres[tt][:, oc * 512:(oc + 1) * 512], in1=bank[:], op=OP.add), r=(bkey, "xr%d" % tt), w=("xr%d" % tt,))
            p.barrier()

        ffn(1)

        for tt in range(8):
            p.emit("gp", lambda e, tt=tt: e.dma_start(out=dout[tt * 128:(tt + 1) * 128, :], in_=xres[tt][:]), r=("xr%d" % tt,), w=("outd",))
        p.final_wait("gp")

        with (
            nc.semaphore("Ssync") as SM_sync,
            nc.semaphore("Sgp") as SM_gp,
            nc.semaphore("Sv") as SM_v,
            nc.semaphore("Ss") as SM_s,
            nc.semaphore("St") as SM_t,
            nc.Block() as block,
        ):
            SEM = {"sync": SM_sync, "gp": SM_gp, "v": SM_v, "s": SM_s, "t": SM_t}

            def run(engine, name):
                for st in p.steps[name]:
                    if st[0] == "w":
                        engine.wait_ge(SEM[st[1]], st[2])
                    else:
                        st[1](engine).then_inc(SEM[name], UNIT[name])

            @block.sync
            def _(sync):
                run(sync, "sync")

            @block.gpsimd
            def _(gp):
                run(gp, "gp")

            @block.vector
            def _(v):
                run(v, "v")

            @block.scalar
            def _(s):
                run(s, "s")

            @block.tensor
            def _(t):
                run(t, "t")
    return nc


_NC_CACHE = None


def kernel(**inp):
    global _NC_CACHE
    x = np.ascontiguousarray(np.asarray(inp["L0_attn_a_hidden_states"], dtype=np.float32))

    def tile_w_in_out(W):
        return np.ascontiguousarray(W.reshape(16, 128, 4, 512).transpose(2, 1, 0, 3))

    wa = np.stack([
        np.ascontiguousarray(np.asarray(inp["L0_attn_a_W" + nm], np.float32).reshape(16, 128, 16, 128).transpose(2, 1, 0, 3))
        for nm in ("q", "k", "v")])
    woa = tile_w_in_out(np.asarray(inp["L0_attn_a_Wo"], np.float32))

    dn = np.arange(128)
    rho = np.where(dn < 64, 2 * dn, 2 * (dn - 64) + 1)
    hp = np.arange(16)
    pi = (hp % 4) * 4 + hp // 4
    perm_q = (pi[:, None] * 128 + rho[None, :]).reshape(-1)
    perm_k = (hp[:, None] * 128 + rho[None, :]).reshape(-1)
    Wqb = np.asarray(inp["L0_attn_b_Wq"], np.float32)
    Wkb = np.asarray(inp["L0_attn_b_Wk"], np.float32)
    Wvb = np.asarray(inp["L0_attn_b_Wv"], np.float32)
    Wob = np.asarray(inp["L0_attn_b_Wo"], np.float32)
    wb = np.stack([
        tile_w_in_out(np.ascontiguousarray(Wqb[perm_q, :].T)),
        tile_w_in_out(np.ascontiguousarray(Wkb[perm_k, :].T)),
        tile_w_in_out(np.ascontiguousarray(Wvb.T)),
        tile_w_in_out(np.ascontiguousarray(Wob.T)),
    ])

    nmap = np.concatenate([2 * np.arange(8192), 2 * np.arange(8192) + 1])

    def tile_gu_pk(Wp):
        Tm = np.ascontiguousarray(np.asarray(Wp, np.int32).T)[:, nmap]
        return np.ascontiguousarray(Tm.reshape(8, 128, 128, 128).transpose(2, 1, 0, 3))

    def tile_gu_sc(sc):
        Am = np.asarray(sc, np.float32)[nmap, :]
        C = np.repeat(np.ascontiguousarray(Am.T), 16, axis=0)
        return np.ascontiguousarray(C.reshape(8, 128, 128, 128).transpose(2, 1, 0, 3))

    gpk = tile_gu_pk(inp["L0_ffn_gate_W_packed"])
    upk = tile_gu_pk(inp["L0_ffn_up_W_packed"])
    gsc = tile_gu_sc(inp["L0_ffn_gate_scales"])
    usc = tile_gu_sc(inp["L0_ffn_up_scales"])
    dpk = np.ascontiguousarray(np.asarray(inp["L0_ffn_down_W_packed"], np.int32).T).reshape(64, 128, 2048)
    dsc = np.ascontiguousarray(np.repeat(np.ascontiguousarray(np.asarray(inp["L0_ffn_down_scales"], np.float32).T), 16, axis=0)).reshape(64, 128, 2048)

    inv = 10000.0 ** (-(np.arange(0, 128, 2, dtype=np.float64) / 128.0))
    ident = np.eye(128, dtype=np.float32)
    cstv = np.tile(np.array([-4.0, -6.0, 0.5, 1e-5, 1.0, 0.0, ISQ, -2.0], dtype=np.float32), (128, 1))

    if _NC_CACHE is None:
        _NC_CACHE = build_nc()
    nc = _NC_CACHE

    in_maps = []
    for c in range(8):
        b, pos0 = c // 4, (c % 4) * 1024
        xo = np.ascontiguousarray(x[b, pos0:pos0 + 1024])
        xh = np.ascontiguousarray(x[b, pos0 - 256:pos0]) if pos0 > 0 else np.zeros((256, 2048), np.float32)
        kpos = np.maximum(np.arange(pos0 - 256, pos0 + 1024, dtype=np.float64), 0.0)
        ang = inv[:, None] * kpos[None, :]
        cosA = np.cos(ang).astype(np.float32)
        sinA = np.sin(ang).astype(np.float32)
        qpos = np.arange(pos0, pos0 + 1024, dtype=np.float64)
        angB = qpos[:, None] * inv[None, :]
        cosB = np.ascontiguousarray(np.cos(angB).astype(np.float32).reshape(8, 128, 64).transpose(1, 0, 2))
        sinB = np.ascontiguousarray(np.sin(angB).astype(np.float32).reshape(8, 128, 64).transpose(1, 0, 2))
        mask = np.zeros((4, 128, 512), np.float32)
        first = pos0 == 0
        for qs in range(2):
            qq = qs * 128 + np.arange(128)
            cz = np.where(np.arange(256)[None, :] <= qq[:, None], 0.0, NEG).astype(np.float32)
            mask[qs, :, 256:512] = cz
            mask[2 + qs, :, 256:512] = cz
            mask[qs, :, 0:256] = NEG if first else 0.0
            mask[2 + qs, :, 0:256] = 0.0
        in_maps.append({
            "x": xo, "xh": xh, "wa": wa, "woa": woa, "wb": wb,
            "gpk": gpk, "upk": upk, "gsc": gsc, "usc": usc, "dpk": dpk, "dsc": dsc,
            "mask": mask, "cosA": cosA, "sinA": sinA, "cosB": cosB, "sinB": sinB,
            "ident": ident, "cst": cstv,
        })
    res = run_bass_kernel_spmd(nc, in_maps, core_ids=list(range(8)))
    out = np.zeros((2, 4096, 2048), np.float32)
    for c in range(8):
        b, pos0 = c // 4, (c % 4) * 1024
        out[b, pos0:pos0 + 1024] = res.results[c]["o"]
    return out
